# version 40
# baseline (speedup 1.0000x reference)
import numpy as np

import concourse.bass as bass
import concourse.mybir as mybir
from concourse.bass_utils import run_bass_kernel_spmd

F32 = mybir.dt.float32
BF16 = mybir.dt.bfloat16
AF = mybir.ActivationFunctionType
ALU = mybir.AluOpType
AX = mybir.AxisListType

NCORES = 8
D = 1024
CH = 8
NH = 16
HD = 64
E = 8
EXP_BIAS = -8.0
LN_EPS = 1e-5
SB_LIMIT = 212863


class Sched:
    ENGINES = ("sync", "gpsimd", "tensor", "vector", "scalar")

    def __init__(self, nc):
        self.nc = nc
        self.ops = []
        self.cnt = {}
        self.sems = {}

    def val(self, sem):
        return self.cnt.get(id(sem), 0)

    def ev(self, sem):
        return (sem, self.val(sem))

    def op(self, engine, fn, waits=(), inc=None, inc_n=1):
        ws = []
        for w in waits:
            if w is None:
                continue
            s, v = w
            if v <= 0:
                continue
            assert v <= self.val(s), f"wait on future sem value {v} > {self.val(s)}"
            ws.append((s, v))
        if inc is not None:
            self.cnt[id(inc)] = self.cnt.get(id(inc), 0) + inc_n
        self.ops.append((engine, fn, ws, inc, inc_n))
        return (inc, self.cnt[id(inc)]) if inc is not None else None

    def drain(self, engine):
        self.ops.append((engine, lambda e: e.drain(), [], None, 1))

    def finalize(self, block):
        def mk(name):
            def body(eng):
                last = {}
                for (e, fn, waits, inc, inc_n) in self.ops:
                    if e != name:
                        continue
                    for (s, v) in waits:
                        if last.get(id(s), 0) < v:
                            eng.wait_ge(s, v)
                            last[id(s)] = v
                    inst = fn(eng)
                    if inc is not None:
                        assert inst is not None
                        inst.then_inc(inc, inc_n)
            return body
        for name in self.ENGINES:
            getattr(block, name)(mk(name))


def ap(handle, offset, dims):
    return bass.AP(handle, offset, [list(d) for d in dims])


def build_nc(T):
    NT = T // 512
    KC = T // 128
    NS = T // 128
    assert T % 512 == 0

    nc = bass.Bass("TRN2", target_bir_lowering=False)

    d_xT = nc.dram_tensor("xT", [D, T], F32, kind="ExternalInput")
    d_wqkv = nc.dram_tensor("w_qkv", [D, 3 * D], F32, kind="ExternalInput")
    d_wproj = nc.dram_tensor("w_proj", [D, D], F32, kind="ExternalInput")
    d_bprojT = nc.dram_tensor("b_projT", [1, D], F32, kind="ExternalInput")
    d_rwT = nc.dram_tensor("r_wT", [D, E], F32, kind="ExternalInput")
    d_rbT = nc.dram_tensor("router_bT", [1, E], F32, kind="ExternalInput")
    d_ewT = nc.dram_tensor("expert_wT", [E, D, D], F32, kind="ExternalInput")
    d_eb = nc.dram_tensor("expert_b", [E, D], F32, kind="ExternalInput")
    d_cos = nc.dram_tensor("cosx", [128, T], F32, kind="ExternalInput")
    d_sin = nc.dram_tensor("sinx", [128, T], F32, kind="ExternalInput")
    d_r2 = nc.dram_tensor("r2", [128, 128], F32, kind="ExternalInput")
    d_id = nc.dram_tensor("ident", [128, 128], F32, kind="ExternalInput")
    d_onesc = nc.dram_tensor("onesc", [128, 1], F32, kind="ExternalInput")
    d_ones11 = nc.dram_tensor("ones11", [1, 1], F32, kind="ExternalInput")
    d_onesk1 = nc.dram_tensor("onesk1", [1, 128], F32, kind="ExternalInput")
    d_ones65 = nc.dram_tensor("ones65", [65, 128], F32, kind="ExternalInput")
    d_sel2 = nc.dram_tensor("sel2", [33, 128], F32, kind="ExternalInput")
    d_onesrow = nc.dram_tensor("onesrow", [1, 512], F32, kind="ExternalInput")
    d_cbh = nc.dram_tensor("cbh", [E, T], BF16)
    d_out = nc.dram_tensor("out", [D, T], F32, kind="ExternalOutput")

    x_all = nc.alloc_sbuf_tensor("x_all", [128, CH * T], F32)
    xn = nc.alloc_sbuf_tensor("xn", [128, CH * T], BF16)
    cb_fm = nc.alloc_sbuf_tensor("cb_fm", [E, T], BF16)
    rows_f = nc.alloc_sbuf_tensor("rows_f", [1, 512], F32)
    rows_b = nc.alloc_sbuf_tensor("rows_b", [1, 1024], BF16)
    rb_bc = nc.alloc_sbuf_tensor("rb_bc", [128, E], F32)
    lg_sb = nc.alloc_sbuf_tensor("lg_sb", [128, 2 * E], F32)
    e_sb = nc.alloc_sbuf_tensor("e_sb", [128, 2 * E], F32)
    onehot = nc.alloc_sbuf_tensor("onehot", [128, 2 * E], F32)
    coeff = nc.alloc_sbuf_tensor("coeff", [128, 2 * E], BF16)
    lmax = nc.alloc_sbuf_tensor("lmax", [128, 2], F32)
    nmax = nc.alloc_sbuf_tensor("nmax", [128, 2], F32)
    ssum = nc.alloc_sbuf_tensor("ssum", [128, 2], F32)
    rinv = nc.alloc_sbuf_tensor("rinv", [128, 2], F32)
    eb_sb = nc.alloc_sbuf_tensor("eb_sb", [E, D], BF16)
    bproj_sb = nc.alloc_sbuf_tensor("bproj_sb", [1, D], BF16)
    rwt_sb = nc.alloc_sbuf_tensor("rwt_sb", [128, CH * E], F32)
    rbrow = nc.alloc_sbuf_tensor("rbrow", [1, E], F32)
    r2_bf = nc.alloc_sbuf_tensor("r2_bf", [128, 128], BF16)
    id_bf = nc.alloc_sbuf_tensor("id_bf", [128, 128], BF16)
    onesc_bf = nc.alloc_sbuf_tensor("onesc_bf", [128, 1], BF16)
    onesc_f = nc.alloc_sbuf_tensor("onesc_f", [128, 1], F32)
    ones11_bf = nc.alloc_sbuf_tensor("ones11_bf", [1, 1], BF16)
    onesk1_bf = nc.alloc_sbuf_tensor("onesk1_bf", [1, 128], BF16)
    onesk1_f = nc.alloc_sbuf_tensor("onesk1_f", [1, 128], F32)
    onesrow_bf = nc.alloc_sbuf_tensor("onesrow_bf", [1, 512], BF16)
    expb_c = nc.alloc_sbuf_tensor("expb_c", [128, 1], F32)
    rn2_sb = nc.alloc_sbuf_tensor("rn2_sb", [128, 512], BF16)
    sums2 = nc.alloc_sbuf_tensor("sums2", [33, 512], F32)
    rinv2 = nc.alloc_sbuf_tensor("rinv2", [33, 512], BF16)
    ones65_bf = nc.alloc_sbuf_tensor("ones65_bf", [65, 128], BF16)
    sel2_bf = nc.alloc_sbuf_tensor("sel2_bf", [33, 128], BF16)
    lntmp = [nc.alloc_sbuf_tensor(f"lntmp{i}", [128, 512], F32) for i in range(2)]

    static_used = SB_LIMIT - nc.sbuf_bytes_remaining
    SB_BASE = 16512
    SB_TOP = 229376
    AR0 = (SB_BASE + static_used + 95) // 32 * 32
    AR_SZ = SB_TOP - AR0

    def ar(name, shape, dtype, off):
        szb = shape[1] * (2 if dtype == BF16 else 4)
        assert off + szb <= AR_SZ, f"{name}: arena overflow {off + szb} > {AR_SZ}"
        return nc.alloc_sbuf_tensor_at(name, shape, dtype, offset=AR0 + off)

    W_sb = [ar(f"W{i}", [128, CH * D], BF16, i * 16384) for i in range(3)]
    x_e = [ar(f"xe{i}", [128, CH * 512], BF16, 49152 + i * 8192) for i in range(3)]
    cb_bc = [ar(f"cb{i}", [128, T], BF16, 73728 + i * 2 * T) for i in range(2)]
    sq = [ar(f"sq{i}", [128, 512], BF16, 73728 + 4 * T + i * 1024) for i in range(2)]
    cosx = ar("cosx_a", [128, T], BF16, 0)
    sinx = ar("sinx_a", [128, T], BF16, 2 * T)
    wq_sb = [ar(f"wq{i}", [128, CH * 384], BF16, 4 * T + i * CH * 768) for i in range(2)]
    off0 = 4 * T + 2 * CH * 768
    qsb = [ar(f"qsb{i}", [128, T], BF16, off0 + i * 2 * T) for i in range(2)]
    ksb = [ar(f"ksb{i}", [128, T], BF16, off0 + (2 + i) * 2 * T) for i in range(2)]
    vsb = ar("vsb", [128, T], BF16, off0 + 8 * T)
    off1 = off0 + 10 * T
    va_stride = (KC * 260 + 31) // 32 * 32
    vaug = [ar(f"vaug{i}", [128, KC * 130], BF16, off1 + i * va_stride) for i in range(2)]
    off2 = off1 + 2 * va_stride
    P_sb = [ar(f"P{i}", [128, 1024], BF16, off2 + i * 2048) for i in range(3)]
    off3 = off2 + 6144
    attn_fm = ar("attn_fm", [128, CH * T], BF16, off3)
    assert off3 + CH * T * 2 <= AR_SZ, "attn arena overflow"
    wproj_sb = ar("wproj_a", [128, CH * D], BF16, 0)

    PS = nc.alloc_psum_tensor("PS", [128, 4096], F32)

    def bank(i, w=512, p0=0, p1=128, col=0):
        return PS[p0:p1, i * 512 + col:i * 512 + col + w]

    import contextlib
    stack = contextlib.ExitStack()
    dS = stack.enter_context(nc.semaphore("dS"))
    dG = stack.enter_context(nc.semaphore("dG"))
    sT = stack.enter_context(nc.semaphore("sT"))
    sV = stack.enter_context(nc.semaphore("sV"))
    sS = stack.enter_context(nc.semaphore("sS"))
    S = Sched(nc)

    def gdma(dst, src, waits=()):
        S.op("gpsimd", lambda g, d=dst, s_=src: g.dma_start(d, s_),
             waits=waits, inc=dG, inc_n=16)
        v = S.val(dG)
        S.op("gpsimd", lambda g, v=v: g.wait_ge(dG, v))
        return S.ev(dG)

    def sdma(dst, src, waits=()):
        S.op("sync", lambda g, d=dst, s_=src: g.dma_start(d, s_),
             waits=waits, inc=dS, inc_n=16)
        v = S.val(dS)
        S.op("sync", lambda g, v=v: g.wait_ge(dS, v))
        return S.ev(dS)

    sdma(rwt_sb[:, :], ap(d_rwT, 0, [[E, 128], [128 * E, CH], [1, E]]))
    sdma(rbrow[:, :], d_rbT[:, :])
    sdma(onesk1_f[:, :], d_onesk1[:, :])
    ev_setup_s = sdma(onesc_f[:, :], d_onesc[:, :])
    ev_xc = {}
    for c_ in range(CH):
        sdma(ap(x_all, c_ * T, [[CH * T, 128], [1, T]]),
             ap(d_xT, c_ * 128 * T, [[T, 128], [1, T]]))
        ev_xc[c_] = S.ev(dS)
    ev_x0 = ev_xc

    gdma(eb_sb[:, :], d_eb[:, :])
    gdma(bproj_sb[:, :], d_bprojT[:, :])
    gdma(r2_bf[:, :], d_r2[:, :])
    gdma(id_bf[:, :], d_id[:, :])
    gdma(onesc_bf[:, :], d_onesc[:, :])
    gdma(ones11_bf[:, :], d_ones11[:, :])
    gdma(onesk1_bf[:, :], d_onesk1[:, :])
    gdma(onesrow_bf[:, :], d_onesrow[:, :])
    gdma(ones65_bf[:, :], d_ones65[:, :])
    ev_setup_g = gdma(sel2_bf[:, :], d_sel2[:, :])

    from concourse.hw_specs import get_activation_tables
    set_idx = None
    for idx, funcs in enumerate(get_activation_tables(nc.m.arch).values()):
        fl = list(funcs)
        if AF.Exp in fl and AF.Ln in fl:
            set_idx = idx
            break
    assert set_idx is not None
    S.op("scalar", lambda sc: sc.add_instruction(mybir.InstLoadActFuncSet(
        name=nc.get_next_instruction_name(), act_func_set_id=set_idx,
        ins=[], outs=[])), inc=sS)

    S.op("vector", lambda v: v.memset(expb_c[:, :], EXP_BIAS), inc=sV)
    S.op("vector", lambda v: v.memset(sums2[:, :], 1.0), inc=sV)
    ev_expb = S.ev(sV)
    S.op("tensor", lambda t: t.matmul(bank(0, w=E), onesk1_f[:, :], rbrow[:, :],
                                      start=True, stop=True),
         waits=[ev_setup_s], inc=sT)
    S.op("vector", lambda v: v.tensor_copy(rb_bc[:, :], bank(0, w=E)),
         waits=[S.ev(sT)], inc=sV)
    ev_rbbc = S.ev(sV)

    def emit_ln(with_router, ev_x_ready, ev_xn_free, ev_psT, ev_psV):
        ev_rawc = {}
        ev_atm = {}
        ev_apply = {}
        ev_sqmm = {}
        prev_ab = None
        prev_bc = None
        prev_var = None
        prev_rows = None
        F32R = mybir.dt.float32r
        for t in range(NT):
            for c in range(CH):
                w0 = ([ev_x_ready.get(c)] if isinstance(ev_x_ready, dict)
                      else [])
                if t == 0 and c == 0:
                    w0 += [ev_xn_free, ev_psT, ev_psV]
                    if not isinstance(ev_x_ready, dict):
                        w0.append(ev_x_ready)
                wsq = list(w0)
                if (t, c - 2) in ev_sqmm:
                    wsq.append(ev_sqmm[(t, c - 2)])
                elif c < 2 and (t - 1, CH - 2 + c) in ev_sqmm:
                    wsq.append(ev_sqmm[(t - 1, CH - 2 + c)])
                S.op("vector", lambda v, c=c, t=t: v.tensor_copy(
                    xn[:, c * T + t * 512: c * T + t * 512 + 512],
                    x_all[:, c * T + t * 512: c * T + t * 512 + 512]),
                    waits=w0, inc=sV)
                cast_ev = S.ev(sV)
                S.op("scalar", lambda s, c=c, t=t: s.activation(
                    sq[c % 2][:, 0:512],
                    x_all[:, c * T + t * 512: c * T + t * 512 + 512], AF.Square),
                    waits=wsq, inc=sS)
                sq_ev = S.ev(sS)
                S.op("tensor", lambda tn, c=c, t=t: tn.matmul(
                    bank(0, p0=0, p1=1), onesc_bf[:, :],
                    xn[:, c * T + t * 512: c * T + t * 512 + 512],
                    start=(c == 0), stop=(c == CH - 1)),
                    waits=[cast_ev] + ([prev_var, ev_psT] if c == 0 else []) +
                    ([ev_setup_g, ev_setup_s, ev_rbbc] if (t == 0 and c == 0) else []),
                    inc=(sT if c == CH - 1 else None))
                S.op("tensor", lambda tn, c=c, t=t: tn.matmul(
                    bank(1, p0=0, p1=1), onesc_bf[:, :],
                    sq[c % 2][:, 0:512],
                    start=(c == 0), stop=(c == CH - 1)),
                    waits=[sq_ev] + ([prev_var] if c == 0 else []), inc=sT)
                ev_sqmm[(t, c)] = S.ev(sT)
            ev_stats = S.ev(sT)
            mr = PS[32:33, 2 * 512:2 * 512 + 512]
            qr = PS[32:33, 3 * 512:3 * 512 + 512]
            srow = rows_f[0:1, 0:512]
            S.op("vector", lambda v: v.tensor_scalar_mul(mr, bank(0, p0=0, p1=1), 1.0 / D),
                 waits=[ev_stats, prev_ab, prev_bc], inc=sV)
            ev_m = S.ev(sV)
            S.op("scalar", lambda s: s.activation(srow, mr, AF.Square),
                 waits=[ev_m], inc=sS)
            ev_sq2 = S.ev(sS)
            S.op("vector", lambda v: v.scalar_tensor_tensor(
                srow, bank(1, p0=0, p1=1), 1.0 / D, srow, ALU.mult, ALU.subtract),
                waits=[ev_sq2])
            S.drain("vector")
            S.op("vector", lambda v: v.tensor_scalar_add(srow, srow, LN_EPS), inc=sV)
            ev_var = S.ev(sV)
            prev_var = ev_var
            S.op("scalar", lambda s: s.activation(qr, srow, AF.Ln),
                 waits=[ev_var], inc=sS)
            S.drain("scalar")
            S.op("scalar", lambda s: s.activation(rows_b[0:1, 512:1024], qr, AF.Exp,
                                                  scale=-0.5), inc=sS)
            ev_ab = S.ev(sS)
            prev_ab = ev_ab
            S.op("vector", lambda v: v.tensor_copy(rows_b[0:1, 0:512], mr),
                 waits=[ev_ab], inc=sV)
            ev_mrow = S.ev(sV)
            prev_rows = ev_mrow
            if with_router:
                for s_ in range(4):
                    S.op("tensor", lambda tn, s_=s_, t=t: tn.matmul(
                        bank(4 + s_, w=1, col=256 + 2 * t),
                        rows_b[0:1, 512 + s_ * 128: 512 + (s_ + 1) * 128],
                        ones11_bf[:, :], start=True, stop=True),
                        waits=[ev_ab] + ([ev_setup_g] if (t == 0 and s_ == 0) else []),
                        inc=sT)
                    ev_atm[(t, s_)] = S.ev(sT)
            S.op("tensor", lambda tn: tn.matmul(bank(2), onesk1_bf[:, :],
                                                rows_b[0:1, 0:512],
                                                start=True, stop=True),
                 waits=[ev_mrow] + ([ev_setup_g] if t == 0 else []), inc=sT)
            S.op("tensor", lambda tn: tn.matmul(bank(3), onesk1_bf[:, :],
                                                rows_b[0:1, 512:1024],
                                                start=True, stop=True),
                 waits=[ev_ab], inc=sT)
            ev_bc = S.ev(sT)
            prev_bc = ev_bc
            for c in range(CH):
                w = [ev_bc]
                if with_router:
                    if (t, c - 2) in ev_rawc:
                        w.append(ev_rawc[(t, c - 2)])
                    if c < 2 and (t - 1, CH - 2 + c) in ev_rawc:
                        w.append(ev_rawc[(t - 1, CH - 2 + c)])
                S.op("vector", lambda v, c=c, t=t: v.tensor_sub(
                    lntmp[c % 2][:, :],
                    x_all[:, c * T + t * 512: c * T + t * 512 + 512], bank(2)),
                    waits=w, inc=sV)
                ev_p1 = S.ev(sV)
                S.drain("vector")
                S.op("vector", lambda v, c=c, t=t: v.tensor_mul(
                    xn[:, c * T + t * 512: c * T + t * 512 + 512],
                    lntmp[c % 2][:, :], bank(3)), inc=sV)
                ev_apply[(t, c)] = S.ev(sV)
                if with_router:
                    for s_ in range(4):
                        S.op("tensor", lambda tn, c=c, t=t, s_=s_: tn.matmul(
                            bank(4 + s_, w=E, col=t * E),
                            lntmp[c % 2][:, s_ * 128:(s_ + 1) * 128],
                            rwt_sb[:, c * E:(c + 1) * E],
                            start=(c == 0), stop=(c == CH - 1)),
                            waits=[ev_p1] + ([ev_psT] if (t == 0 and c == 0) else []),
                            inc=(sT if s_ == 3 else None))
                    ev_rawc[(t, c)] = S.ev(sT)
        return {"apply": ev_apply, "rawc": ev_rawc, "atm": ev_atm}

    def emit_router(ln_ev, ev_cbh_consumed):
        last_cb = None
        stA = {}

        def stageA(ts):
            t, s_ = ts // 4, ts % 4
            p = ts % 2
            _ = evC
            lg = lg_sb[:, p * E:(p + 1) * E]
            S.op("vector", lambda v, t=t, s_=s_, lg=lg: v.scalar_tensor_tensor(
                lg, bank(4 + s_, w=E, col=t * E),
                bank(4 + s_, w=1, col=256 + 2 * t), rb_bc[:, :],
                ALU.mult, ALU.add),
                waits=[ln_ev["rawc"][(t, CH - 1)], ln_ev["atm"][(t, s_)], ev_rbbc]
                + ([evC[ts - 2]] if ts - 2 in evC else []),
                inc=sV)
            S.drain("vector")
            S.op("vector", lambda v, lg=lg, p=p: v.reduce_max(
                lmax[:, p:p + 1], lg, axis=AX.X))
            S.drain("vector")
            S.op("vector", lambda v, p=p: v.tensor_scalar_mul(
                nmax[:, p:p + 1], lmax[:, p:p + 1], -1.0))
            S.op("vector", lambda v, lg=lg, p=p: v.tensor_single_scalar(
                onehot[:, p * E:(p + 1) * E], lg, lmax[:, p:p + 1], ALU.is_ge),
                inc=sV)
            stA[ts] = S.ev(sV)

        def stageC(ts):
            p = ts % 2
            S.op("scalar", lambda s, p=p: s.activation(
                e_sb[:, p * E:(p + 1) * E], lg_sb[:, p * E:(p + 1) * E],
                AF.Exp, bias=nmax[:, p:p + 1]),
                waits=[stA[ts]], inc=sS)
            return S.ev(sS)

        evC = {}

        def stageD(ts):
            nonlocal last_cb
            p = ts % 2
            S.op("vector", lambda v, p=p: v.reduce_sum(
                ssum[:, p:p + 1], e_sb[:, p * E:(p + 1) * E], axis=AX.X),
                waits=[evC[ts]])
            S.drain("vector")

            def _recip(v, p=p):
                with nc.allow_low_precision(reason="gate recip"):
                    return v.reciprocal(rinv[:, p:p + 1], ssum[:, p:p + 1])
            S.op("vector", _recip)
            S.drain("vector")
            S.op("vector", lambda v, p=p: v.tensor_scalar(
                coeff[:, p * E:(p + 1) * E], onehot[:, p * E:(p + 1) * E],
                rinv[:, p:p + 1], 0.5, ALU.mult, ALU.mult), inc=sV)
            ev_c = S.ev(sV)
            S.op("tensor", lambda tn, p=p: tn.transpose(
                PS[0:8, p * 512:p * 512 + 64].bitcast(BF16),
                coeff[:, p * E:(p + 1) * E], id_bf[:, :]),
                waits=[ev_c] + ([ev_setup_g] if ts == 0 else []), inc=sT)
            ev_t = S.ev(sT)
            S.op("vector", lambda v, ts=ts, p=p: v.tensor_copy(
                cb_fm[0:8, ts * 128:(ts + 1) * 128],
                PS[0:8, p * 512:p * 512 + 64].bitcast(BF16)),
                waits=[ev_t], inc=sV)
            last_cb = S.ev(sV)

        for ts in range(NS):
            stageA(ts)
            evC[ts] = stageC(ts)
            if ts >= 1:
                stageD(ts - 1)
        stageD(NS - 1)
        ev = sdma(d_cbh[:, :], cb_fm[:, :], waits=[last_cb, ev_cbh_consumed])
        return ev, last_cb

    def make_ffn_w(slot_order, slot_release):
        st = {"ev_W": {}, "slot_last": {k: list(v) for k, v in slot_release.items()},
              "slot_order": slot_order}

        def issue_w(e):
            slot = slot_order[e % 3]
            for half in range(2):
                src = ap(d_ewT, e * D * D + half * 512,
                         [[D, 128], [128 * D, CH], [1, 512]])
                dst = ap(W_sb[slot], half * 512, [[CH * D, 128], [D, CH], [1, 512]])
                gdma(dst, src, waits=st["slot_last"].get(slot, []) if half == 0 else [])
                st["ev_W"][(e, half)] = S.ev(dG)
        st["issue_w"] = issue_w
        return st

    def emit_ffn(wst, ln_ev, ev_cbh, ev_cbfm, ev_psT, ev_psV, cb_release):
        slot_order = wst["slot_order"]
        ev_W = wst["ev_W"]
        slot_last = wst["slot_last"]
        issue_w = wst["issue_w"]
        ev_xe = {}
        ev_group = {}
        ev_evac = {}
        ev_cbload = {}
        cb_last = [list(cb_release[0]), list(cb_release[1])]
        xe_slot = {}
        xe_last = {}

        def issue_cb(e):
            sdma(ap(cb_bc[e % 2], 0, [[T, 128], [1, T]]),
                 ap(d_cbh, e * T, [[0, 128], [1, T]]),
                 waits=[ev_cbh] + cb_last[e % 2])
            ev_cbload[e] = S.ev(dS)

        issue_cb(0)
        issue_cb(1)

        for pair in range(4):
            e0, e1 = 2 * pair, 2 * pair + 1
            for t in range(NT):
                for ei, e in enumerate((e0, e1)):
                    slot3 = (2 * t + ei) % 3
                    xe_slot[(e, t)] = slot3
                    w = [ev_cbload[e], ln_ev["apply"][(t, CH - 1)]]
                    if slot3 in xe_last:
                        w.append(xe_last[slot3])
                    S.op("vector", lambda v, e=e, t=t, slot3=slot3: v.tensor_tensor(
                        ap(x_e[slot3], 0, [[CH * 512, 128], [512, CH], [1, 512]]),
                        ap(xn, t * 512, [[CH * T, 128], [T, CH], [1, 512]]),
                        ap(cb_bc[e % 2], t * 512, [[T, 128], [0, CH], [1, 512]]),
                        ALU.mult),
                        waits=w, inc=sV)
                    ev_xe[(e, t)] = S.ev(sV)
                for half in range(2):
                    pf = half * 2048
                    prev = ev_evac.get((pair, t - 1, half)) or \
                        ev_evac.get((pair - 1, NT - 1, half))
                    for e in (e0, e1):
                        slot = slot_order[e % 3]
                        nmm = 0
                        for ol in range(4):
                            o = half * 4 + ol
                            for d_ in range(CH):
                                first_e = (nmm == 0)
                                nmm += 1
                                w = []
                                if first_e:
                                    w = [ev_W[(e, half)], ev_xe[(e, t)]]
                                    if e == e0:
                                        w.append(prev)
                                        if pair == 0 and t == 0:
                                            w += [ev_psT, ev_psV]
                                start = (e == e0 and d_ == 0)
                                stop = (e == e1 and d_ == CH - 1)
                                last = (stop and ol == 3)
                                S.op("tensor",
                                     lambda tn, slot=slot, d_=d_, o=o, e=e, t=t,
                                     pf=pf, start=start, stop=stop:
                                     tn.matmul(
                                         PS[:, pf + (o % 4) * 512: pf + (o % 4) * 512 + 512],
                                         W_sb[slot][:, d_ * D + o * 128: d_ * D + o * 128 + 128],
                                         x_e[xe_slot[(e, t)]][:, d_ * 512:(d_ + 1) * 512],
                                         start=start, stop=stop),
                                     waits=w, inc=(sT if last else None))
                        if e == e0 and pair == 0:
                            for ol in range(4):
                                o = half * 4 + ol
                                S.op("tensor", lambda tn, o=o, t=t, pf=pf: tn.matmul(
                                    PS[:, pf + (o % 4) * 512: pf + (o % 4) * 512 + 512],
                                    eb_sb[0:E, o * 128:(o + 1) * 128],
                                    cb_fm[0:E, t * 512:(t + 1) * 512],
                                    start=False, stop=False),
                                    waits=[ev_setup_g, ev_cbfm] if (o % 4) == 0 else [])
                    ev_group[(pair, t, half)] = S.ev(sT)
                    S.op("vector", lambda v, t=t, half=half, pf=pf: v.tensor_tensor(
                        ap(x_all, half * 4 * T + t * 512,
                           [[CH * T, 128], [T, 4], [1, 512]]),
                        ap(x_all, half * 4 * T + t * 512,
                           [[CH * T, 128], [T, 4], [1, 512]]),
                        ap(PS, pf, [[4096, 128], [512, 4], [1, 512]]),
                        ALU.add),
                        waits=[ev_group[(pair, t, half)]], inc=sV)
                    ev_evac[(pair, t, half)] = S.ev(sV)
                xe_last[xe_slot[(e0, t)]] = ev_group[(pair, t, 1)]
                xe_last[xe_slot[(e1, t)]] = ev_group[(pair, t, 1)]
            last_ev = ev_group[(pair, NT - 1, 1)]
            slot_last[slot_order[e0 % 3]] = [last_ev]
            slot_last[slot_order[e1 % 3]] = [last_ev]
            cb_last[e0 % 2] = [ev_xe[(e0, NT - 1)]]
            cb_last[e1 % 2] = [ev_xe[(e1, NT - 1)]]
            if pair < 3:
                issue_w(2 * pair + 3)
                if 2 * pair + 4 < E:
                    issue_w(2 * pair + 4)
                issue_cb(2 * pair + 2)
                if 2 * pair + 3 < E:
                    issue_cb(2 * pair + 3)
        return {"evac": ev_evac, "last": S.ev(sV), "lastT": S.ev(sT),
                "group": ev_group}

    def emit_attention(ln_ev, ev_arenaT, ev_arenaV, ev_psT, ev_psV):
        gdma(cosx[:, :], d_cos[:, :], waits=[ev_arenaT, ev_arenaV])
        ev_tab = gdma(sinx[:, :], d_sin[:, :])

        for i in range(2):
            S.op("vector", lambda v, i=i: v.memset(vaug[i][:, :], 1.0),
                 waits=[ev_arenaT, ev_arenaV] if i == 0 else [], inc=sV)
        ev_vaug_init = S.ev(sV)

        NP = NH // 2
        wq_release = {}
        qk_release = {}
        vaug_release = {}
        ev_prep = {}
        ev_wdma = {}
        psP_free = {0: [ev_psT, ev_psV]}
        psN_free = [[ev_psT, ev_psV]]
        psS_free = {0: ev_psV, 1: ev_psV}
        psO_free = {0: [ev_psV], 1: [ev_psV]}
        rope_done = {}
        ev_vt = {}
        wq_last = {}
        st = {"attnfm": None}

        def issue_wq(hp):
            ws = hp % 2
            w = wq_release.get(ws, [ev_arenaT, ev_arenaV])
            for j in range(3):
                src = ap(d_wqkv, j * D + hp * 128,
                         [[3 * D, 128], [128 * 3 * D, CH], [1, 128]])
                dst = ap(wq_sb[ws], j * 128, [[CH * 384, 128], [384, CH], [1, 128]])
                gdma(dst, src, waits=w if j == 0 else [])
            ev_wdma[hp] = S.ev(dG)

        def prep_gen(hp):
            ws = hp % 2
            ev_w = ev_wdma[hp]
            gidx = 0
            ev_qkv_mm = None
            for j in range(3):
                dstb = [qsb[ws], ksb[ws], vsb][j]
                relw = qk_release.get(ws, []) if j == 0 else []
                for tt in range(NT):
                    half = 0
                    pb = bank(7)
                    fw = [ev_w, ln_ev["apply"][(tt, CH - 1)]] + psP_free[half] + relw
                    for d_ in range(CH):
                        S.op("tensor", lambda tn, j=j, tt=tt, d_=d_, ws=ws, pb=pb:
                             tn.matmul(pb,
                                       wq_sb[ws][:, d_ * 384 + j * 128: d_ * 384 + j * 128 + 128],
                                       xn[:, d_ * T + tt * 512: d_ * T + tt * 512 + 512],
                                       start=(d_ == 0), stop=(d_ == CH - 1)),
                             waits=fw if d_ == 0 else [],
                             inc=(sT if d_ == CH - 1 else None))
                    ev_qkv_mm = S.ev(sT)
                    S.op("vector", lambda v, dstb=dstb, tt=tt, pb=pb: v.tensor_copy(
                        dstb[:, tt * 512:(tt + 1) * 512], pb),
                        waits=[ev_qkv_mm], inc=sV)
                    psP_free[half] = [S.ev(sV)]
                    ev_prep[(hp, j, tt)] = psP_free[half][0]
                    gidx += 1
                    yield
            wq_last[hp] = ev_qkv_mm
            for j in range(2):
                dstb = [qsb[ws], ksb[ws]][j]
                for tt in range(NT):
                    half = 0
                    pb = bank(7)
                    S.op("tensor", lambda tn, dstb=dstb, tt=tt, pb=pb: tn.matmul(
                        pb, r2_bf[:, :], dstb[:, tt * 512:(tt + 1) * 512],
                        start=True, stop=True),
                        waits=[ev_prep[(hp, j, tt)], ev_setup_g] + psP_free[half],
                        inc=sT)
                    ev_rot = S.ev(sT)
                    S.drain("vector")
                    S.op("vector", lambda v, tt=tt, pb=pb: v.tensor_mul(
                        lntmp[0][:, :], pb, sinx[:, tt * 512:(tt + 1) * 512]),
                        waits=[ev_rot, ev_tab], inc=sV)
                    S.op("vector", lambda v, dstb=dstb, tt=tt: v.tensor_mul(
                        lntmp[1][:, :], dstb[:, tt * 512:(tt + 1) * 512],
                        cosx[:, tt * 512:(tt + 1) * 512]))
                    S.drain("vector")
                    S.op("vector", lambda v, dstb=dstb, tt=tt: v.tensor_add(
                        dstb[:, tt * 512:(tt + 1) * 512], lntmp[0][:, :],
                        lntmp[1][:, :]), inc=sV)
                    psP_free[half] = [S.ev(sV)]
                    rope_done[(hp, j, tt)] = S.ev(sV)
                    gidx += 1
                    yield
            vw = vaug_release.get(ws, [ev_vaug_init])
            for kc in range(KC):
                half = 0
                pcol = 7 * 512
                S.op("tensor", lambda tn, kc=kc, pcol=pcol: tn.transpose(
                    PS[:, pcol:pcol + 64].bitcast(BF16),
                    vsb[:, kc * 128:(kc + 1) * 128], id_bf[:, :]),
                    waits=[ev_prep[(hp, 2, NT - 1)], ev_setup_g] + psP_free[half] +
                    (vw if kc == 0 else []),
                    inc=sT)
                ev_t = S.ev(sT)
                S.op("vector", lambda v, kc=kc, ws=ws, pcol=pcol: v.tensor_copy(
                    ap(vaug[ws], kc * 130, [[KC * 130, 128], [65, 2], [1, 64]]),
                    PS[:, pcol:pcol + 64].bitcast(BF16).rearrange(
                        "p (two x) -> p two x", two=2)),
                    waits=[ev_t, ev_vaug_init], inc=sV)
                psP_free[half] = [S.ev(sV)]
                ev_vt[(hp, kc)] = S.ev(sV)
                gidx += 1
                if kc % 2 == 1:
                    yield

        def emit_inner(hp, driver, pend_norm):
            ws = hp % 2
            P_read = {0: None, 1: None, 2: None}
            ev_pv_h = {}
            pend = None
            for qt in range(NT):
                for kcp in range(KC // 2):
                    ev_sc = {}
                    for k2 in range(2):
                        for h in range(2):
                            base = 64 * h
                            sb2 = h
                            w_sc = [rope_done[(hp, 0, qt)], psS_free[sb2]] + \
                                   [rope_done[(hp, 1, k_)] for k_ in range(NT)]
                            kc = 2 * kcp + k2
                            S.op("tensor", lambda tn, kc=kc, k2=k2, qt=qt, ws=ws,
                                 base=base, sb2=sb2:
                                 tn.matmul(
                                     PS[:, (2 * sb2 + k2) * 512:
                                        (2 * sb2 + k2) * 512 + 512],
                                     ksb[ws][base:base + 64, kc * 128:(kc + 1) * 128],
                                     qsb[ws][base:base + 64, qt * 512:(qt + 1) * 512],
                                     start=True, stop=True),
                                 waits=w_sc if k2 == 0 else [],
                                 inc=(sT if k2 == 1 else None))
                            if k2 == 1:
                                ev_sc[h] = S.ev(sT)

                    if pend_norm is not None and kcp >= 3:
                        pend_norm()
                        pend_norm = None
                    if pend is not None:
                        pend()
                        pend = None
                    for h in range(2):
                        pb_i = (2 * (qt * (KC // 2) + kcp) + h) % 3
                        S.op("scalar", lambda s, h=h, pb_i=pb_i: s.activation(
                            P_sb[pb_i][:, :], bank(2 * h, w=1024),
                            AF.Exp, bias=expb_c[:, :], scale=0.125),
                            waits=[ev_sc[h], P_read[pb_i], ev_expb], inc=sS)
                        psS_free[h] = S.ev(sS)
                    if driver is not None:
                        next(driver, None)
                    ev_exp = {0: psS_free[0], 1: psS_free[1]}

                    def mk_pv(kcp=kcp, ws=ws, qt=qt, ev_exp=dict(ev_exp), hp=hp):
                        def emit_pv():
                            for h in range(2):
                                po = 4 + h
                                pb_i = (2 * (qt * (KC // 2) + kcp) + h) % 3
                                for k2 in range(2):
                                    kc = 2 * kcp + k2
                                    first = (kcp == 0 and k2 == 0)
                                    last = (kcp == KC // 2 - 1 and k2 == 1)
                                    w_pv = [ev_exp[h], ev_vt[(hp, kc)]]
                                    if first:
                                        w_pv += psO_free[h]
                                    S.op("tensor", lambda tn, kc=kc, h=h, ws=ws,
                                         po=po, pb_i=pb_i, k2=k2, first=first,
                                         last=last:
                                         tn.matmul(
                                             bank(po, p0=0, p1=65),
                                             vaug[ws][:, kc * 130 + 65 * h:
                                                      kc * 130 + 65 * h + 65],
                                             P_sb[pb_i][:, k2 * 512:(k2 + 1) * 512],
                                             start=first, stop=last),
                                         waits=w_pv, inc=(sT if k2 == 1 else None))
                                P_read[pb_i] = S.ev(sT)
                                if kcp == KC // 2 - 1:
                                    ev_pv_h[h] = S.ev(sT)
                        return emit_pv
                    pend = mk_pv()
                if pend is not None:
                    pend()
                    pend = None
                ev_evac_h = {}
                ev_ln_h = {}
                for h in range(2):
                    base = 64 * h
                    S.op("vector", lambda v, h=h, base=base, hp=hp, qt=qt:
                         v.tensor_copy(
                             attn_fm[base:base + 64,
                                     hp * T + qt * 512: hp * T + qt * 512 + 512],
                             bank(4 + h, p0=0, p1=64)),
                         waits=[ev_pv_h[h]], inc=sV)
                    ev_evac_h[h] = S.ev(sV)
                for h in range(2):
                    S.op("scalar", lambda s, h=h: s.activation(
                        sums2[32 * h:32 * h + 1, :],
                        bank(4 + h, p0=64, p1=65), AF.Ln),
                        waits=[ev_pv_h[h]], inc=sS)
                    ev_ln_h[h] = S.ev(sS)
                for h in range(2):
                    psO_free[h] = [ev_evac_h[h], ev_ln_h[h]]
                ev_ln2 = ev_ln_h[1]

                def mk_norm(qt=qt, hp=hp, ev_ln2=ev_ln2):
                    def emit_norm():
                        S.op("scalar", lambda s: s.activation(
                            rinv2[0:33, :], sums2[0:33, :], AF.Exp, scale=-1.0),
                            waits=[ev_ln2], inc=sS)
                        ev_ri = S.ev(sS)
                        for h in range(2):
                            S.op("tensor", lambda tn, h=h: tn.matmul(
                                PS[h * 64:(h + 1) * 64, 6 * 512:7 * 512],
                                sel2_bf[0:33, h * 64:h * 64 + 64],
                                rinv2[0:33, :],
                                start=True, stop=True),
                                waits=[ev_ri, ev_setup_g] +
                                (psN_free[0] if h == 0 else []),
                                inc=(sT if h == 1 else None))
                        ev_b = S.ev(sT)
                        S.op("vector", lambda v: v.tensor_copy(
                            rn2_sb[:, :], PS[:, 6 * 512:7 * 512]),
                            waits=[ev_b], inc=sV)
                        psN_free[0] = [S.ev(sV)]
                        for h in range(2):
                            base = 64 * h
                            sl = attn_fm[base:base + 64,
                                         hp * T + qt * 512: hp * T + qt * 512 + 512]
                            S.op("vector", lambda v, sl=sl, base=base:
                                 v.tensor_mul(sl, sl, rn2_sb[base:base + 64, :]),
                                 waits=[], inc=sV)
                            st["attnfm"] = S.ev(sV)
                    return emit_norm
                pend_norm = mk_norm()
            wq_release[ws] = [wq_last[hp]]
            qk_release[ws] = [S.ev(sT)]
            vaug_release[ws] = [S.ev(sT)]
            return pend_norm

        issue_wq(0)
        g = prep_gen(0)
        for _ in g:
            pass
        pn = None
        for hp in range(NP):
            if hp + 1 < NP:
                issue_wq(hp + 1)
                nxt = prep_gen(hp + 1)
            else:
                nxt = None
            pn = emit_inner(hp, nxt, pn)
            if nxt is not None:
                for _ in nxt:
                    pass
        if pn is not None:
            pn()
        last_attnfm = st["attnfm"]
        last_psP = list(psP_free[0])

        gdma(ap(wproj_sb, 0, [[CH * D, 128], [D, CH], [1, 1024]]),
             ap(d_wproj, 0, [[D, 128], [128 * D, CH], [1, 1024]]),
             waits=qk_release[0] + qk_release[1] + [S.ev(sV)])
        ev_wp = S.ev(dG)
        ev_evac = {}
        for o in range(CH):
            pf = (o % 2) * 2048
            for t in range(NT):
                fw = [ev_wp, last_attnfm]
                if (o - 2, t) in ev_evac:
                    fw.append(ev_evac[(o - 2, t)])
                if o < 2 and t == 0:
                    fw += [psS_free[0], psS_free[1]]
                    fw += psO_free[0] + psO_free[1] + psN_free[0]
                    fw += last_psP
                for d_ in range(CH):
                    S.op("tensor", lambda tn, o=o, t=t, d_=d_, pf=pf: tn.matmul(
                        PS[:, pf + t * 512: pf + t * 512 + 512],
                        wproj_sb[:, d_ * D + o * 128: d_ * D + o * 128 + 128],
                        attn_fm[:, d_ * T + t * 512: d_ * T + t * 512 + 512],
                        start=(d_ == 0), stop=False),
                        waits=fw if d_ == 0 else [])
                S.op("tensor", lambda tn, o=o, t=t, pf=pf: tn.matmul(
                    PS[:, pf + t * 512: pf + t * 512 + 512],
                    bproj_sb[0:1, o * 128:(o + 1) * 128],
                    onesrow_bf[0:1, :], start=False, stop=True),
                    waits=[ev_setup_g], inc=sT)
                ev_mm = S.ev(sT)
                S.op("vector", lambda v, o=o, t=t, pf=pf: v.tensor_tensor(
                    x_all[:, o * T + t * 512: o * T + t * 512 + 512],
                    x_all[:, o * T + t * 512: o * T + t * 512 + 512],
                    PS[:, pf + t * 512: pf + t * 512 + 512], ALU.add),
                    waits=[ev_mm], inc=sV)
                ev_evac[(o, t)] = S.ev(sV)
        return {"last": S.ev(sV), "lastT": S.ev(sT)}

    Z = None

    w1 = make_ffn_w([0, 1, 2], {})
    for e in range(3):
        w1["issue_w"](e)
    ln1 = emit_ln(True, ev_x0, Z, Z, Z)
    ev_cbh1, ev_cbfm1 = emit_router(ln1, Z)
    ffn1 = emit_ffn(w1, ln1, ev_cbh1, ev_cbfm1, Z, Z, [[], []])

    ln2 = emit_ln(False, ffn1["last"], ffn1["lastT"], ffn1["lastT"], ffn1["last"])
    at = emit_attention(ln2, ffn1["lastT"], ffn1["last"], ffn1["lastT"],
                        ffn1["last"])

    w2 = make_ffn_w([1, 2, 0],
                    {0: [at["lastT"], at["last"]], 1: [at["lastT"], at["last"]],
                     2: [at["lastT"], at["last"]]})
    ln3 = emit_ln(True, at["last"], at["lastT"], at["lastT"], at["last"])
    for e in range(3):
        w2["issue_w"](e)
    ev_cbh2, ev_cbfm2 = emit_router(ln3, S.ev(dS))
    ffn2 = emit_ffn(w2, ln3, ev_cbh2, ev_cbfm2, at["lastT"], at["last"],
                    [[at["lastT"], at["last"]], [at["lastT"], at["last"]]])

    for t in range(NT):
        for half in range(2):
            sdma(ap(d_out, half * 4 * 128 * T + t * 512,
                    [[T, 128], [128 * T, 4], [1, 512]]),
                 ap(x_all, half * 4 * T + t * 512,
                    [[CH * T, 128], [T, 4], [1, 512]]),
                 waits=[ffn2["evac"][(3, t, half)]])
    final = S.ev(dS)
    S.op("sync", lambda e: e.nop(), waits=[final])

    with nc.Block() as block:
        S.finalize(block)
    stack.close()
    return nc


def _sel33():
    s = np.zeros((33, 128), np.float32)
    s[0, 0:64] = 1.0
    s[32, 64:128] = 1.0
    return s


def _consts(T):
    inv = 1.0 / (10000.0 ** (np.arange(0, HD, 2, dtype=np.float64) / HD))
    t = np.arange(T, dtype=np.float64)
    fr = t[:, None] * inv[None, :]
    cos = np.cos(fr).astype(np.float32)
    sin = np.sin(fr).astype(np.float32)
    cosx = np.empty((128, T), np.float32)
    sinx = np.empty((128, T), np.float32)
    for b in range(4):
        cosx[b * 32:(b + 1) * 32, :] = cos.T
        sinx[b * 32:(b + 1) * 32, :] = sin.T
    r2 = np.zeros((128, 128), np.float32)
    for blk in range(2):
        o = blk * 64
        for m in range(32):
            r2[o + m + 32, o + m] = -1.0
            r2[o + m, o + m + 32] = 1.0
    return {
        "cosx": cosx, "sinx": sinx, "r2": r2,
        "ident": np.eye(128, dtype=np.float32),
        "onesc": np.ones((128, 1), np.float32),
        "ones11": np.ones((1, 1), np.float32),
        "onesk1": np.ones((1, 128), np.float32),
        "ones65": np.ones((65, 128), np.float32),
        "sel2": _sel33(),
        "onesrow": np.ones((1, 512), np.float32),
    }


def kernel(L0_attn_x, L0_attn_w_qkv, L0_attn_w_proj, L0_attn_b_proj,
           L0_ffn_router_w, L0_ffn_router_b, L0_ffn_expert_w, L0_ffn_expert_b):
    T = L0_attn_x.shape[1]
    nc = build_nc(T)
    xT = np.ascontiguousarray(np.asarray(L0_attn_x).transpose(0, 2, 1))
    shared = {
        "w_qkv": np.ascontiguousarray(np.asarray(L0_attn_w_qkv)),
        "w_proj": np.ascontiguousarray(np.asarray(L0_attn_w_proj)),
        "b_projT": np.ascontiguousarray(np.asarray(L0_attn_b_proj).reshape(1, D)),
        "r_wT": np.ascontiguousarray(np.asarray(L0_ffn_router_w).T),
        "router_bT": np.ascontiguousarray(np.asarray(L0_ffn_router_b).reshape(1, E)),
        "expert_wT": np.ascontiguousarray(np.asarray(L0_ffn_expert_w).transpose(0, 2, 1)),
        "expert_b": np.ascontiguousarray(np.asarray(L0_ffn_expert_b)),
    }
    shared.update(_consts(T))
    in_maps = []
    for b in range(NCORES):
        m = dict(shared)
        m["xT"] = np.ascontiguousarray(xT[b])
        in_maps.append(m)
    res = run_bass_kernel_spmd(nc, in_maps, core_ids=list(range(NCORES)))
    out = np.stack([np.asarray(r["out"]).T for r in res.results], axis=0)
    return np.ascontiguousarray(out.astype(np.float32))


# version 41
# speedup vs baseline: 1.0033x; 1.0033x over previous
import numpy as np

import concourse.bass as bass
import concourse.mybir as mybir
from concourse.bass_utils import run_bass_kernel_spmd

F32 = mybir.dt.float32
BF16 = mybir.dt.bfloat16
AF = mybir.ActivationFunctionType
ALU = mybir.AluOpType
AX = mybir.AxisListType

NCORES = 8
D = 1024
CH = 8
NH = 16
HD = 64
E = 8
EXP_BIAS = -8.0
LN_EPS = 1e-5
SB_LIMIT = 212863


class Sched:
    ENGINES = ("sync", "gpsimd", "tensor", "vector", "scalar")

    def __init__(self, nc):
        self.nc = nc
        self.ops = []
        self.cnt = {}
        self.sems = {}

    def val(self, sem):
        return self.cnt.get(id(sem), 0)

    def ev(self, sem):
        return (sem, self.val(sem))

    def op(self, engine, fn, waits=(), inc=None, inc_n=1):
        ws = []
        for w in waits:
            if w is None:
                continue
            s, v = w
            if v <= 0:
                continue
            assert v <= self.val(s), f"wait on future sem value {v} > {self.val(s)}"
            ws.append((s, v))
        if inc is not None:
            self.cnt[id(inc)] = self.cnt.get(id(inc), 0) + inc_n
        self.ops.append((engine, fn, ws, inc, inc_n))
        return (inc, self.cnt[id(inc)]) if inc is not None else None

    def drain(self, engine):
        self.ops.append((engine, lambda e: e.drain(), [], None, 1))

    def finalize(self, block):
        def mk(name):
            def body(eng):
                last = {}
                for (e, fn, waits, inc, inc_n) in self.ops:
                    if e != name:
                        continue
                    for (s, v) in waits:
                        if last.get(id(s), 0) < v:
                            eng.wait_ge(s, v)
                            last[id(s)] = v
                    inst = fn(eng)
                    if inc is not None:
                        assert inst is not None
                        inst.then_inc(inc, inc_n)
            return body
        for name in self.ENGINES:
            getattr(block, name)(mk(name))


def ap(handle, offset, dims):
    return bass.AP(handle, offset, [list(d) for d in dims])


def build_nc(T):
    NT = T // 512
    KC = T // 128
    NS = T // 128
    assert T % 512 == 0

    nc = bass.Bass("TRN2", target_bir_lowering=False)

    d_xT = nc.dram_tensor("xT", [D, T], F32, kind="ExternalInput")
    d_wqkv = nc.dram_tensor("w_qkv", [D, 3 * D], F32, kind="ExternalInput")
    d_wproj = nc.dram_tensor("w_proj", [D, D], F32, kind="ExternalInput")
    d_bprojT = nc.dram_tensor("b_projT", [1, D], F32, kind="ExternalInput")
    d_rwT = nc.dram_tensor("r_wT", [D, E], F32, kind="ExternalInput")
    d_rbT = nc.dram_tensor("router_bT", [1, E], F32, kind="ExternalInput")
    d_ewT = nc.dram_tensor("expert_wT", [E, D, D], F32, kind="ExternalInput")
    d_eb = nc.dram_tensor("expert_b", [E, D], F32, kind="ExternalInput")
    d_cos = nc.dram_tensor("cosx", [128, T], F32, kind="ExternalInput")
    d_sin = nc.dram_tensor("sinx", [128, T], F32, kind="ExternalInput")
    d_r2 = nc.dram_tensor("r2", [128, 128], F32, kind="ExternalInput")
    d_id = nc.dram_tensor("ident", [128, 128], F32, kind="ExternalInput")
    d_onesc = nc.dram_tensor("onesc", [128, 1], F32, kind="ExternalInput")
    d_ones11 = nc.dram_tensor("ones11", [1, 1], F32, kind="ExternalInput")
    d_onesk1 = nc.dram_tensor("onesk1", [1, 128], F32, kind="ExternalInput")
    d_ones65 = nc.dram_tensor("ones65", [65, 128], F32, kind="ExternalInput")
    d_sel2 = nc.dram_tensor("sel2", [33, 128], F32, kind="ExternalInput")
    d_onesrow = nc.dram_tensor("onesrow", [1, 512], F32, kind="ExternalInput")
    d_cbh = nc.dram_tensor("cbh", [E, T], BF16)
    d_out = nc.dram_tensor("out", [D, T], F32, kind="ExternalOutput")

    x_all = nc.alloc_sbuf_tensor("x_all", [128, CH * T], F32)
    xn = nc.alloc_sbuf_tensor("xn", [128, CH * T], BF16)
    cb_fm = nc.alloc_sbuf_tensor("cb_fm", [E, T], BF16)
    rows_f = nc.alloc_sbuf_tensor("rows_f", [1, 512], F32)
    rows_b = nc.alloc_sbuf_tensor("rows_b", [1, 1024], BF16)
    rb_bc = nc.alloc_sbuf_tensor("rb_bc", [128, E], F32)
    lg_sb = nc.alloc_sbuf_tensor("lg_sb", [128, 2 * E], F32)
    e_sb = nc.alloc_sbuf_tensor("e_sb", [128, 2 * E], F32)
    onehot = nc.alloc_sbuf_tensor("onehot", [128, 2 * E], F32)
    coeff = nc.alloc_sbuf_tensor("coeff", [128, 2 * E], BF16)
    lmax = nc.alloc_sbuf_tensor("lmax", [128, 2], F32)
    nmax = nc.alloc_sbuf_tensor("nmax", [128, 2], F32)
    ssum = nc.alloc_sbuf_tensor("ssum", [128, 2], F32)
    rinv = nc.alloc_sbuf_tensor("rinv", [128, 2], F32)
    eb_sb = nc.alloc_sbuf_tensor("eb_sb", [E, D], BF16)
    bproj_sb = nc.alloc_sbuf_tensor("bproj_sb", [1, D], BF16)
    rwt_sb = nc.alloc_sbuf_tensor("rwt_sb", [128, CH * E], F32)
    rbrow = nc.alloc_sbuf_tensor("rbrow", [1, E], F32)
    r2_bf = nc.alloc_sbuf_tensor("r2_bf", [128, 128], BF16)
    id_bf = nc.alloc_sbuf_tensor("id_bf", [128, 128], BF16)
    onesc_bf = nc.alloc_sbuf_tensor("onesc_bf", [128, 1], BF16)
    onesc_f = nc.alloc_sbuf_tensor("onesc_f", [128, 1], F32)
    ones11_bf = nc.alloc_sbuf_tensor("ones11_bf", [1, 1], BF16)
    onesk1_bf = nc.alloc_sbuf_tensor("onesk1_bf", [1, 128], BF16)
    onesk1_f = nc.alloc_sbuf_tensor("onesk1_f", [1, 128], F32)
    onesrow_bf = nc.alloc_sbuf_tensor("onesrow_bf", [1, 512], BF16)
    expb_c = nc.alloc_sbuf_tensor("expb_c", [128, 1], F32)
    rn2_sb = nc.alloc_sbuf_tensor("rn2_sb", [128, 512], BF16)
    sums2 = nc.alloc_sbuf_tensor("sums2", [33, 512], F32)
    rinv2 = nc.alloc_sbuf_tensor("rinv2", [33, 512], BF16)
    ones65_bf = nc.alloc_sbuf_tensor("ones65_bf", [65, 128], BF16)
    sel2_bf = nc.alloc_sbuf_tensor("sel2_bf", [33, 128], BF16)
    lntmp = [nc.alloc_sbuf_tensor(f"lntmp{i}", [128, 512], F32) for i in range(2)]

    static_used = SB_LIMIT - nc.sbuf_bytes_remaining
    SB_BASE = 16512
    SB_TOP = 229376
    AR0 = (SB_BASE + static_used + 95) // 32 * 32
    AR_SZ = SB_TOP - AR0

    def ar(name, shape, dtype, off):
        szb = shape[1] * (2 if dtype == BF16 else 4)
        assert off + szb <= AR_SZ, f"{name}: arena overflow {off + szb} > {AR_SZ}"
        return nc.alloc_sbuf_tensor_at(name, shape, dtype, offset=AR0 + off)

    W_sb = [ar(f"W{i}", [128, CH * D], BF16, i * 16384) for i in range(3)]
    x_e = [ar(f"xe{i}", [128, CH * 512], BF16, 49152 + i * 8192) for i in range(3)]
    cb_bc = [ar(f"cb{i}", [128, T], BF16, 73728 + i * 2 * T) for i in range(2)]
    sq = [ar(f"sq{i}", [128, 512], BF16, 73728 + 4 * T + i * 1024) for i in range(2)]
    cosx = ar("cosx_a", [128, T], BF16, 0)
    sinx = ar("sinx_a", [128, T], BF16, 2 * T)
    wq_sb = [ar(f"wq{i}", [128, CH * 384], BF16, 4 * T + i * CH * 768) for i in range(2)]
    off0 = 4 * T + 2 * CH * 768
    qsb = [ar(f"qsb{i}", [128, T], BF16, off0 + i * 2 * T) for i in range(2)]
    ksb = [ar(f"ksb{i}", [128, T], BF16, off0 + (2 + i) * 2 * T) for i in range(2)]
    vsb = ar("vsb", [128, T], BF16, off0 + 8 * T)
    off1 = off0 + 10 * T
    va_stride = (KC * 260 + 31) // 32 * 32
    vaug = [ar(f"vaug{i}", [128, KC * 130], BF16, off1 + i * va_stride) for i in range(2)]
    off2 = off1 + 2 * va_stride
    P_sb = [ar(f"P{i}", [128, 1024], BF16, off2 + i * 2048) for i in range(3)]
    off3 = off2 + 6144
    attn_fm = ar("attn_fm", [128, CH * T], BF16, off3)
    assert off3 + CH * T * 2 <= AR_SZ, "attn arena overflow"
    wproj_sb = ar("wproj_a", [128, CH * D], BF16, 0)

    PS = nc.alloc_psum_tensor("PS", [128, 4096], F32)

    def bank(i, w=512, p0=0, p1=128, col=0):
        return PS[p0:p1, i * 512 + col:i * 512 + col + w]

    import contextlib
    stack = contextlib.ExitStack()
    dS = stack.enter_context(nc.semaphore("dS"))
    dG = stack.enter_context(nc.semaphore("dG"))
    sT = stack.enter_context(nc.semaphore("sT"))
    sV = stack.enter_context(nc.semaphore("sV"))
    sS = stack.enter_context(nc.semaphore("sS"))
    S = Sched(nc)

    def gdma(dst, src, waits=()):
        S.op("gpsimd", lambda g, d=dst, s_=src: g.dma_start(d, s_),
             waits=waits, inc=dG, inc_n=16)
        v = S.val(dG)
        S.op("gpsimd", lambda g, v=v: g.wait_ge(dG, v))
        return S.ev(dG)

    def sdma(dst, src, waits=()):
        S.op("sync", lambda g, d=dst, s_=src: g.dma_start(d, s_),
             waits=waits, inc=dS, inc_n=16)
        v = S.val(dS)
        S.op("sync", lambda g, v=v: g.wait_ge(dS, v))
        return S.ev(dS)

    ev_x0 = sdma(ap(x_all, 0, [[CH * T, 128], [T, CH], [1, T]]),
                 ap(d_xT, 0, [[T, 128], [128 * T, CH], [1, T]]))
    sdma(rwt_sb[:, :], ap(d_rwT, 0, [[E, 128], [128 * E, CH], [1, E]]))
    sdma(rbrow[:, :], d_rbT[:, :])
    sdma(onesk1_f[:, :], d_onesk1[:, :])
    ev_setup_s = sdma(onesc_f[:, :], d_onesc[:, :])

    gdma(eb_sb[:, :], d_eb[:, :])
    gdma(bproj_sb[:, :], d_bprojT[:, :])
    gdma(r2_bf[:, :], d_r2[:, :])
    gdma(id_bf[:, :], d_id[:, :])
    gdma(onesc_bf[:, :], d_onesc[:, :])
    gdma(ones11_bf[:, :], d_ones11[:, :])
    gdma(onesk1_bf[:, :], d_onesk1[:, :])
    gdma(onesrow_bf[:, :], d_onesrow[:, :])
    gdma(ones65_bf[:, :], d_ones65[:, :])
    ev_setup_g = gdma(sel2_bf[:, :], d_sel2[:, :])

    from concourse.hw_specs import get_activation_tables
    set_idx = None
    for idx, funcs in enumerate(get_activation_tables(nc.m.arch).values()):
        fl = list(funcs)
        if AF.Exp in fl and AF.Ln in fl:
            set_idx = idx
            break
    assert set_idx is not None
    S.op("scalar", lambda sc: sc.add_instruction(mybir.InstLoadActFuncSet(
        name=nc.get_next_instruction_name(), act_func_set_id=set_idx,
        ins=[], outs=[])), inc=sS)

    S.op("vector", lambda v: v.memset(expb_c[:, :], EXP_BIAS), inc=sV)
    S.op("vector", lambda v: v.memset(sums2[:, :], 1.0), inc=sV)
    ev_expb = S.ev(sV)
    S.op("tensor", lambda t: t.matmul(bank(0, w=E), onesk1_f[:, :], rbrow[:, :],
                                      start=True, stop=True),
         waits=[ev_setup_s], inc=sT)
    S.op("vector", lambda v: v.tensor_copy(rb_bc[:, :], bank(0, w=E)),
         waits=[S.ev(sT)], inc=sV)
    ev_rbbc = S.ev(sV)

    def emit_ln(with_router, ev_x_ready, ev_xn_free, ev_psT, ev_psV):
        ev_rawc = {}
        ev_atm = {}
        ev_apply = {}
        ev_sqmm = {}
        prev_ab = None
        prev_bc = None
        prev_var = None
        prev_rows = None
        F32R = mybir.dt.float32r
        for t in range(NT):
            for c in range(CH):
                w0 = [ev_x_ready, ev_xn_free, ev_psT, ev_psV] if (t == 0 and c == 0) else []
                wsq = list(w0)
                if (t, c - 2) in ev_sqmm:
                    wsq.append(ev_sqmm[(t, c - 2)])
                elif c < 2 and (t - 1, CH - 2 + c) in ev_sqmm:
                    wsq.append(ev_sqmm[(t - 1, CH - 2 + c)])
                S.op("vector", lambda v, c=c, t=t: v.tensor_copy(
                    xn[:, c * T + t * 512: c * T + t * 512 + 512],
                    x_all[:, c * T + t * 512: c * T + t * 512 + 512]),
                    waits=w0, inc=sV)
                cast_ev = S.ev(sV)
                S.op("scalar", lambda s, c=c, t=t: s.activation(
                    sq[c % 2][:, 0:512],
                    x_all[:, c * T + t * 512: c * T + t * 512 + 512], AF.Square),
                    waits=wsq, inc=sS)
                sq_ev = S.ev(sS)
                S.op("tensor", lambda tn, c=c, t=t: tn.matmul(
                    bank(0, p0=0, p1=1), onesc_bf[:, :],
                    xn[:, c * T + t * 512: c * T + t * 512 + 512],
                    start=(c == 0), stop=(c == CH - 1)),
                    waits=[cast_ev] + ([prev_var, ev_psT] if c == 0 else []) +
                    ([ev_setup_g, ev_setup_s, ev_rbbc] if (t == 0 and c == 0) else []),
                    inc=(sT if c == CH - 1 else None))
                S.op("tensor", lambda tn, c=c, t=t: tn.matmul(
                    bank(1, p0=0, p1=1), onesc_bf[:, :],
                    sq[c % 2][:, 0:512],
                    start=(c == 0), stop=(c == CH - 1)),
                    waits=[sq_ev] + ([prev_var] if c == 0 else []), inc=sT)
                ev_sqmm[(t, c)] = S.ev(sT)
            ev_stats = S.ev(sT)
            mr = PS[32:33, 2 * 512:2 * 512 + 512]
            qr = PS[32:33, 3 * 512:3 * 512 + 512]
            srow = rows_f[0:1, 0:512]
            S.op("vector", lambda v: v.tensor_scalar_mul(mr, bank(0, p0=0, p1=1), 1.0 / D),
                 waits=[ev_stats, prev_ab, prev_bc], inc=sV)
            ev_m = S.ev(sV)
            S.op("scalar", lambda s: s.activation(srow, mr, AF.Square),
                 waits=[ev_m], inc=sS)
            ev_sq2 = S.ev(sS)
            S.op("vector", lambda v: v.scalar_tensor_tensor(
                srow, bank(1, p0=0, p1=1), 1.0 / D, srow, ALU.mult, ALU.subtract),
                waits=[ev_sq2])
            S.drain("vector")
            S.op("vector", lambda v: v.tensor_scalar_add(srow, srow, LN_EPS), inc=sV)
            ev_var = S.ev(sV)
            prev_var = ev_var
            S.op("scalar", lambda s: s.activation(qr, srow, AF.Ln),
                 waits=[ev_var], inc=sS)
            S.drain("scalar")
            S.op("scalar", lambda s: s.activation(rows_b[0:1, 512:1024], qr, AF.Exp,
                                                  scale=-0.5), inc=sS)
            ev_ab = S.ev(sS)
            prev_ab = ev_ab
            S.op("vector", lambda v: v.tensor_copy(rows_b[0:1, 0:512], mr),
                 waits=[ev_ab], inc=sV)
            ev_mrow = S.ev(sV)
            prev_rows = ev_mrow
            if with_router:
                for s_ in range(4):
                    S.op("tensor", lambda tn, s_=s_, t=t: tn.matmul(
                        bank(4 + s_, w=1, col=256 + 2 * t),
                        rows_b[0:1, 512 + s_ * 128: 512 + (s_ + 1) * 128],
                        ones11_bf[:, :], start=True, stop=True),
                        waits=[ev_ab] + ([ev_setup_g] if (t == 0 and s_ == 0) else []),
                        inc=sT)
                    ev_atm[(t, s_)] = S.ev(sT)
            S.op("tensor", lambda tn: tn.matmul(bank(2), onesk1_bf[:, :],
                                                rows_b[0:1, 0:512],
                                                start=True, stop=True),
                 waits=[ev_mrow] + ([ev_setup_g] if t == 0 else []), inc=sT)
            S.op("tensor", lambda tn: tn.matmul(bank(3), onesk1_bf[:, :],
                                                rows_b[0:1, 512:1024],
                                                start=True, stop=True),
                 waits=[ev_ab], inc=sT)
            ev_bc = S.ev(sT)
            prev_bc = ev_bc
            for c in range(CH):
                w = [ev_bc]
                if with_router:
                    if (t, c - 2) in ev_rawc:
                        w.append(ev_rawc[(t, c - 2)])
                    if c < 2 and (t - 1, CH - 2 + c) in ev_rawc:
                        w.append(ev_rawc[(t - 1, CH - 2 + c)])
                S.op("vector", lambda v, c=c, t=t: v.tensor_sub(
                    lntmp[c % 2][:, :],
                    x_all[:, c * T + t * 512: c * T + t * 512 + 512], bank(2)),
                    waits=w, inc=sV)
                ev_p1 = S.ev(sV)
                S.drain("vector")
                S.op("vector", lambda v, c=c, t=t: v.tensor_mul(
                    xn[:, c * T + t * 512: c * T + t * 512 + 512],
                    lntmp[c % 2][:, :], bank(3)), inc=sV)
                ev_apply[(t, c)] = S.ev(sV)
                if with_router:
                    for s_ in range(4):
                        S.op("tensor", lambda tn, c=c, t=t, s_=s_: tn.matmul(
                            bank(4 + s_, w=E, col=t * E),
                            lntmp[c % 2][:, s_ * 128:(s_ + 1) * 128],
                            rwt_sb[:, c * E:(c + 1) * E],
                            start=(c == 0), stop=(c == CH - 1)),
                            waits=[ev_p1] + ([ev_psT] if (t == 0 and c == 0) else []),
                            inc=(sT if s_ == 3 else None))
                    ev_rawc[(t, c)] = S.ev(sT)
        return {"apply": ev_apply, "rawc": ev_rawc, "atm": ev_atm}

    def emit_router(ln_ev, ev_cbh_consumed):
        last_cb = None
        stA = {}

        def stageA(ts):
            t, s_ = ts // 4, ts % 4
            p = ts % 2
            _ = evC
            lg = lg_sb[:, p * E:(p + 1) * E]
            S.op("vector", lambda v, t=t, s_=s_, lg=lg: v.scalar_tensor_tensor(
                lg, bank(4 + s_, w=E, col=t * E),
                bank(4 + s_, w=1, col=256 + 2 * t), rb_bc[:, :],
                ALU.mult, ALU.add),
                waits=[ln_ev["rawc"][(t, CH - 1)], ln_ev["atm"][(t, s_)], ev_rbbc]
                + ([evC[ts - 2]] if ts - 2 in evC else []),
                inc=sV)
            S.drain("vector")
            S.op("vector", lambda v, lg=lg, p=p: v.reduce_max(
                lmax[:, p:p + 1], lg, axis=AX.X))
            S.drain("vector")
            S.op("vector", lambda v, p=p: v.tensor_scalar_mul(
                nmax[:, p:p + 1], lmax[:, p:p + 1], -1.0))
            S.op("vector", lambda v, lg=lg, p=p: v.tensor_single_scalar(
                onehot[:, p * E:(p + 1) * E], lg, lmax[:, p:p + 1], ALU.is_ge),
                inc=sV)
            stA[ts] = S.ev(sV)

        def stageC(ts):
            p = ts % 2
            S.op("scalar", lambda s, p=p: s.activation(
                e_sb[:, p * E:(p + 1) * E], lg_sb[:, p * E:(p + 1) * E],
                AF.Exp, bias=nmax[:, p:p + 1]),
                waits=[stA[ts]], inc=sS)
            return S.ev(sS)

        evC = {}

        def stageD(ts):
            nonlocal last_cb
            p = ts % 2
            S.op("vector", lambda v, p=p: v.reduce_sum(
                ssum[:, p:p + 1], e_sb[:, p * E:(p + 1) * E], axis=AX.X),
                waits=[evC[ts]])
            S.drain("vector")

            def _recip(v, p=p):
                with nc.allow_low_precision(reason="gate recip"):
                    return v.reciprocal(rinv[:, p:p + 1], ssum[:, p:p + 1])
            S.op("vector", _recip)
            S.drain("vector")
            S.op("vector", lambda v, p=p: v.tensor_scalar(
                coeff[:, p * E:(p + 1) * E], onehot[:, p * E:(p + 1) * E],
                rinv[:, p:p + 1], 0.5, ALU.mult, ALU.mult), inc=sV)
            ev_c = S.ev(sV)
            S.op("tensor", lambda tn, p=p: tn.transpose(
                PS[0:8, p * 512:p * 512 + 64].bitcast(BF16),
                coeff[:, p * E:(p + 1) * E], id_bf[:, :]),
                waits=[ev_c] + ([ev_setup_g] if ts == 0 else []), inc=sT)
            ev_t = S.ev(sT)
            S.op("vector", lambda v, ts=ts, p=p: v.tensor_copy(
                cb_fm[0:8, ts * 128:(ts + 1) * 128],
                PS[0:8, p * 512:p * 512 + 64].bitcast(BF16)),
                waits=[ev_t], inc=sV)
            last_cb = S.ev(sV)

        for ts in range(NS):
            stageA(ts)
            evC[ts] = stageC(ts)
            if ts >= 1:
                stageD(ts - 1)
        stageD(NS - 1)
        ev = sdma(d_cbh[:, :], cb_fm[:, :], waits=[last_cb, ev_cbh_consumed])
        return ev, last_cb

    def make_ffn_w(slot_order, slot_release):
        st = {"ev_W": {}, "slot_last": {k: list(v) for k, v in slot_release.items()},
              "slot_order": slot_order}

        def issue_w(e):
            slot = slot_order[e % 3]
            for half in range(2):
                src = ap(d_ewT, e * D * D + half * 512,
                         [[D, 128], [128 * D, CH], [1, 512]])
                dst = ap(W_sb[slot], half * 512, [[CH * D, 128], [D, CH], [1, 512]])
                gdma(dst, src, waits=st["slot_last"].get(slot, []) if half == 0 else [])
                st["ev_W"][(e, half)] = S.ev(dG)
        st["issue_w"] = issue_w
        return st

    def emit_ffn(wst, ln_ev, ev_cbh, ev_cbfm, ev_psT, ev_psV, cb_release):
        slot_order = wst["slot_order"]
        ev_W = wst["ev_W"]
        slot_last = wst["slot_last"]
        issue_w = wst["issue_w"]
        ev_xe = {}
        ev_group = {}
        ev_evac = {}
        ev_cbload = {}
        cb_last = [list(cb_release[0]), list(cb_release[1])]
        xe_slot = {}
        xe_last = {}

        def issue_cb(e):
            sdma(ap(cb_bc[e % 2], 0, [[T, 128], [1, T]]),
                 ap(d_cbh, e * T, [[0, 128], [1, T]]),
                 waits=[ev_cbh] + cb_last[e % 2])
            ev_cbload[e] = S.ev(dS)

        issue_cb(0)
        issue_cb(1)

        for pair in range(4):
            e0, e1 = 2 * pair, 2 * pair + 1
            for t in range(NT):
                for ei, e in enumerate((e0, e1)):
                    slot3 = (2 * t + ei) % 3
                    xe_slot[(e, t)] = slot3
                    w = [ev_cbload[e], ln_ev["apply"][(t, CH - 1)]]
                    if slot3 in xe_last:
                        w.append(xe_last[slot3])
                    S.op("vector", lambda v, e=e, t=t, slot3=slot3: v.tensor_tensor(
                        ap(x_e[slot3], 0, [[CH * 512, 128], [512, CH], [1, 512]]),
                        ap(xn, t * 512, [[CH * T, 128], [T, CH], [1, 512]]),
                        ap(cb_bc[e % 2], t * 512, [[T, 128], [0, CH], [1, 512]]),
                        ALU.mult),
                        waits=w, inc=sV)
                    ev_xe[(e, t)] = S.ev(sV)
                for half in range(2):
                    pf = half * 2048
                    prev = ev_evac.get((pair, t - 1, half)) or \
                        ev_evac.get((pair - 1, NT - 1, half))
                    for e in (e0, e1):
                        slot = slot_order[e % 3]
                        nmm = 0
                        for ol in range(4):
                            o = half * 4 + ol
                            for d_ in range(CH):
                                first_e = (nmm == 0)
                                nmm += 1
                                w = []
                                if first_e:
                                    w = [ev_W[(e, half)], ev_xe[(e, t)]]
                                    if e == e0:
                                        w.append(prev)
                                        if pair == 0 and t == 0:
                                            w += [ev_psT, ev_psV]
                                start = (e == e0 and d_ == 0)
                                stop = (e == e1 and d_ == CH - 1)
                                last = (stop and ol == 3)
                                S.op("tensor",
                                     lambda tn, slot=slot, d_=d_, o=o, e=e, t=t,
                                     pf=pf, start=start, stop=stop:
                                     tn.matmul(
                                         PS[:, pf + (o % 4) * 512: pf + (o % 4) * 512 + 512],
                                         W_sb[slot][:, d_ * D + o * 128: d_ * D + o * 128 + 128],
                                         x_e[xe_slot[(e, t)]][:, d_ * 512:(d_ + 1) * 512],
                                         start=start, stop=stop),
                                     waits=w, inc=(sT if last else None))
                        if e == e0 and pair == 0:
                            for ol in range(4):
                                o = half * 4 + ol
                                S.op("tensor", lambda tn, o=o, t=t, pf=pf: tn.matmul(
                                    PS[:, pf + (o % 4) * 512: pf + (o % 4) * 512 + 512],
                                    eb_sb[0:E, o * 128:(o + 1) * 128],
                                    cb_fm[0:E, t * 512:(t + 1) * 512],
                                    start=False, stop=False),
                                    waits=[ev_setup_g, ev_cbfm] if (o % 4) == 0 else [])
                    ev_group[(pair, t, half)] = S.ev(sT)
                    S.op("vector", lambda v, t=t, half=half, pf=pf: v.tensor_tensor(
                        ap(x_all, half * 4 * T + t * 512,
                           [[CH * T, 128], [T, 4], [1, 512]]),
                        ap(x_all, half * 4 * T + t * 512,
                           [[CH * T, 128], [T, 4], [1, 512]]),
                        ap(PS, pf, [[4096, 128], [512, 4], [1, 512]]),
                        ALU.add),
                        waits=[ev_group[(pair, t, half)]], inc=sV)
                    ev_evac[(pair, t, half)] = S.ev(sV)
                xe_last[xe_slot[(e0, t)]] = ev_group[(pair, t, 1)]
                xe_last[xe_slot[(e1, t)]] = ev_group[(pair, t, 1)]
            last_ev = ev_group[(pair, NT - 1, 1)]
            slot_last[slot_order[e0 % 3]] = [last_ev]
            slot_last[slot_order[e1 % 3]] = [last_ev]
            cb_last[e0 % 2] = [ev_xe[(e0, NT - 1)]]
            cb_last[e1 % 2] = [ev_xe[(e1, NT - 1)]]
            if pair < 3:
                issue_w(2 * pair + 3)
                if 2 * pair + 4 < E:
                    issue_w(2 * pair + 4)
                issue_cb(2 * pair + 2)
                if 2 * pair + 3 < E:
                    issue_cb(2 * pair + 3)
        return {"evac": ev_evac, "last": S.ev(sV), "lastT": S.ev(sT),
                "group": ev_group}

    def emit_attention(ln_ev, ev_arenaT, ev_arenaV, ev_psT, ev_psV):
        gdma(cosx[:, :], d_cos[:, :], waits=[ev_arenaT, ev_arenaV])
        ev_tab = gdma(sinx[:, :], d_sin[:, :])

        for i in range(2):
            S.op("vector", lambda v, i=i: v.memset(vaug[i][:, :], 1.0),
                 waits=[ev_arenaT, ev_arenaV] if i == 0 else [], inc=sV)
        ev_vaug_init = S.ev(sV)

        NP = NH // 2
        wq_release = {}
        qk_release = {}
        vaug_release = {}
        ev_prep = {}
        ev_wdma = {}
        psP_free = {0: [ev_psT, ev_psV]}
        psN_free = [[ev_psT, ev_psV]]
        psS_free = {0: ev_psV, 1: ev_psV}
        psO_free = {0: [ev_psV], 1: [ev_psV]}
        rope_done = {}
        ev_vt = {}
        wq_last = {}
        st = {"attnfm": None}

        def issue_wq(hp):
            ws = hp % 2
            w = wq_release.get(ws, [ev_arenaT, ev_arenaV])
            for j in range(3):
                src = ap(d_wqkv, j * D + hp * 128,
                         [[3 * D, 128], [128 * 3 * D, CH], [1, 128]])
                dst = ap(wq_sb[ws], j * 128, [[CH * 384, 128], [384, CH], [1, 128]])
                gdma(dst, src, waits=w if j == 0 else [])
            ev_wdma[hp] = S.ev(dG)

        def prep_gen(hp):
            ws = hp % 2
            ev_w = ev_wdma[hp]
            gidx = 0
            ev_qkv_mm = None
            for j in range(3):
                dstb = [qsb[ws], ksb[ws], vsb][j]
                relw = qk_release.get(ws, []) if j == 0 else []
                for tt in range(NT):
                    half = 0
                    pb = bank(7)
                    fw = [ev_w, ln_ev["apply"][(tt, CH - 1)]] + psP_free[half] + relw
                    for d_ in range(CH):
                        S.op("tensor", lambda tn, j=j, tt=tt, d_=d_, ws=ws, pb=pb:
                             tn.matmul(pb,
                                       wq_sb[ws][:, d_ * 384 + j * 128: d_ * 384 + j * 128 + 128],
                                       xn[:, d_ * T + tt * 512: d_ * T + tt * 512 + 512],
                                       start=(d_ == 0), stop=(d_ == CH - 1)),
                             waits=fw if d_ == 0 else [],
                             inc=(sT if d_ == CH - 1 else None))
                    ev_qkv_mm = S.ev(sT)
                    S.op("vector", lambda v, dstb=dstb, tt=tt, pb=pb: v.tensor_copy(
                        dstb[:, tt * 512:(tt + 1) * 512], pb),
                        waits=[ev_qkv_mm], inc=sV)
                    psP_free[half] = [S.ev(sV)]
                    ev_prep[(hp, j, tt)] = psP_free[half][0]
                    gidx += 1
                    yield
            wq_last[hp] = ev_qkv_mm
            for j in range(2):
                dstb = [qsb[ws], ksb[ws]][j]
                for tt in range(NT):
                    half = 0
                    pb = bank(7)
                    S.op("tensor", lambda tn, dstb=dstb, tt=tt, pb=pb: tn.matmul(
                        pb, r2_bf[:, :], dstb[:, tt * 512:(tt + 1) * 512],
                        start=True, stop=True),
                        waits=[ev_prep[(hp, j, tt)], ev_setup_g] + psP_free[half],
                        inc=sT)
                    ev_rot = S.ev(sT)
                    S.drain("vector")
                    S.op("vector", lambda v, tt=tt, pb=pb: v.tensor_mul(
                        lntmp[0][:, :], pb, sinx[:, tt * 512:(tt + 1) * 512]),
                        waits=[ev_rot, ev_tab], inc=sV)
                    S.op("vector", lambda v, dstb=dstb, tt=tt: v.tensor_mul(
                        lntmp[1][:, :], dstb[:, tt * 512:(tt + 1) * 512],
                        cosx[:, tt * 512:(tt + 1) * 512]))
                    S.drain("vector")
                    S.op("vector", lambda v, dstb=dstb, tt=tt: v.tensor_add(
                        dstb[:, tt * 512:(tt + 1) * 512], lntmp[0][:, :],
                        lntmp[1][:, :]), inc=sV)
                    psP_free[half] = [S.ev(sV)]
                    rope_done[(hp, j, tt)] = S.ev(sV)
                    gidx += 1
                    yield
            vw = vaug_release.get(ws, [ev_vaug_init])
            for kc in range(KC):
                half = 0
                pcol = 7 * 512
                S.op("tensor", lambda tn, kc=kc, pcol=pcol: tn.transpose(
                    PS[:, pcol:pcol + 64].bitcast(BF16),
                    vsb[:, kc * 128:(kc + 1) * 128], id_bf[:, :]),
                    waits=[ev_prep[(hp, 2, NT - 1)], ev_setup_g] + psP_free[half] +
                    (vw if kc == 0 else []),
                    inc=sT)
                ev_t = S.ev(sT)
                S.op("vector", lambda v, kc=kc, ws=ws, pcol=pcol: v.tensor_copy(
                    ap(vaug[ws], kc * 130, [[KC * 130, 128], [65, 2], [1, 64]]),
                    PS[:, pcol:pcol + 64].bitcast(BF16).rearrange(
                        "p (two x) -> p two x", two=2)),
                    waits=[ev_t, ev_vaug_init], inc=sV)
                psP_free[half] = [S.ev(sV)]
                ev_vt[(hp, kc)] = S.ev(sV)
                gidx += 1
                if kc % 2 == 1:
                    yield

        def emit_inner(hp, driver, pend_norm):
            ws = hp % 2
            P_read = {0: None, 1: None, 2: None}
            ev_pv_h = {}
            pend = None
            for qt in range(NT):
                for kcp in range(KC // 2):
                    ev_sc = {}
                    for k2 in range(2):
                        for h in range(2):
                            base = 64 * h
                            sb2 = h
                            w_sc = [rope_done[(hp, 0, qt)], psS_free[sb2]] + \
                                   [rope_done[(hp, 1, k_)] for k_ in range(NT)]
                            kc = 2 * kcp + k2
                            S.op("tensor", lambda tn, kc=kc, k2=k2, qt=qt, ws=ws,
                                 base=base, sb2=sb2:
                                 tn.matmul(
                                     PS[:, (2 * sb2 + k2) * 512:
                                        (2 * sb2 + k2) * 512 + 512],
                                     ksb[ws][base:base + 64, kc * 128:(kc + 1) * 128],
                                     qsb[ws][base:base + 64, qt * 512:(qt + 1) * 512],
                                     start=True, stop=True),
                                 waits=w_sc if k2 == 0 else [],
                                 inc=(sT if k2 == 1 else None))
                            if k2 == 1:
                                ev_sc[h] = S.ev(sT)

                    if pend_norm is not None and kcp >= 3:
                        pend_norm()
                        pend_norm = None
                    if pend is not None:
                        pend()
                        pend = None
                    for h in range(2):
                        pb_i = (2 * (qt * (KC // 2) + kcp) + h) % 3
                        S.op("scalar", lambda s, h=h, pb_i=pb_i: s.activation(
                            P_sb[pb_i][:, :], bank(2 * h, w=1024),
                            AF.Exp, bias=expb_c[:, :], scale=0.125),
                            waits=[ev_sc[h], P_read[pb_i], ev_expb], inc=sS)
                        psS_free[h] = S.ev(sS)
                    if driver is not None:
                        next(driver, None)
                    ev_exp = {0: psS_free[0], 1: psS_free[1]}

                    def mk_pv(kcp=kcp, ws=ws, qt=qt, ev_exp=dict(ev_exp), hp=hp):
                        def emit_pv():
                            for h in range(2):
                                po = 4 + h
                                pb_i = (2 * (qt * (KC // 2) + kcp) + h) % 3
                                for k2 in range(2):
                                    kc = 2 * kcp + k2
                                    first = (kcp == 0 and k2 == 0)
                                    last = (kcp == KC // 2 - 1 and k2 == 1)
                                    w_pv = [ev_exp[h], ev_vt[(hp, kc)]]
                                    if first:
                                        w_pv += psO_free[h]
                                    S.op("tensor", lambda tn, kc=kc, h=h, ws=ws,
                                         po=po, pb_i=pb_i, k2=k2, first=first,
                                         last=last:
                                         tn.matmul(
                                             bank(po, p0=0, p1=65),
                                             vaug[ws][:, kc * 130 + 65 * h:
                                                      kc * 130 + 65 * h + 65],
                                             P_sb[pb_i][:, k2 * 512:(k2 + 1) * 512],
                                             start=first, stop=last),
                                         waits=w_pv, inc=(sT if k2 == 1 else None))
                                P_read[pb_i] = S.ev(sT)
                                if kcp == KC // 2 - 1:
                                    ev_pv_h[h] = S.ev(sT)
                        return emit_pv
                    pend = mk_pv()
                if pend is not None:
                    pend()
                    pend = None
                ev_evac_h = {}
                ev_ln_h = {}
                for h in range(2):
                    base = 64 * h
                    S.op("vector", lambda v, h=h, base=base, hp=hp, qt=qt:
                         v.tensor_copy(
                             attn_fm[base:base + 64,
                                     hp * T + qt * 512: hp * T + qt * 512 + 512],
                             bank(4 + h, p0=0, p1=64)),
                         waits=[ev_pv_h[h]], inc=sV)
                    ev_evac_h[h] = S.ev(sV)
                for h in range(2):
                    S.op("scalar", lambda s, h=h: s.activation(
                        sums2[32 * h:32 * h + 1, :],
                        bank(4 + h, p0=64, p1=65), AF.Ln),
                        waits=[ev_pv_h[h]], inc=sS)
                    ev_ln_h[h] = S.ev(sS)
                for h in range(2):
                    psO_free[h] = [ev_evac_h[h], ev_ln_h[h]]
                ev_ln2 = ev_ln_h[1]

                def mk_norm(qt=qt, hp=hp, ev_ln2=ev_ln2):
                    def emit_norm():
                        S.op("scalar", lambda s: s.activation(
                            rinv2[0:33, :], sums2[0:33, :], AF.Exp, scale=-1.0),
                            waits=[ev_ln2], inc=sS)
                        ev_ri = S.ev(sS)
                        for h in range(2):
                            S.op("tensor", lambda tn, h=h: tn.matmul(
                                PS[h * 64:(h + 1) * 64, 6 * 512:7 * 512],
                                sel2_bf[0:33, h * 64:h * 64 + 64],
                                rinv2[0:33, :],
                                start=True, stop=True),
                                waits=[ev_ri, ev_setup_g] +
                                (psN_free[0] if h == 0 else []),
                                inc=(sT if h == 1 else None))
                        ev_b = S.ev(sT)
                        S.op("vector", lambda v: v.tensor_copy(
                            rn2_sb[:, :], PS[:, 6 * 512:7 * 512]),
                            waits=[ev_b], inc=sV)
                        psN_free[0] = [S.ev(sV)]
                        for h in range(2):
                            base = 64 * h
                            sl = attn_fm[base:base + 64,
                                         hp * T + qt * 512: hp * T + qt * 512 + 512]
                            S.op("vector", lambda v, sl=sl, base=base:
                                 v.tensor_mul(sl, sl, rn2_sb[base:base + 64, :]),
                                 waits=[], inc=sV)
                            st["attnfm"] = S.ev(sV)
                    return emit_norm
                pend_norm = mk_norm()
            wq_release[ws] = [wq_last[hp]]
            qk_release[ws] = [S.ev(sT)]
            vaug_release[ws] = [S.ev(sT)]
            return pend_norm

        import itertools
        issue_wq(0)
        g = prep_gen(0)
        for _ in range(20):
            next(g)
        pn = None
        for hp in range(NP):
            if hp + 1 < NP:
                issue_wq(hp + 1)
                nxt = prep_gen(hp + 1)
            else:
                nxt = None
            if hp == 0:
                nxt = itertools.chain(g, nxt) if nxt is not None else g
            pn = emit_inner(hp, nxt, pn)
            if nxt is not None:
                for _ in nxt:
                    pass
        if pn is not None:
            pn()
        last_attnfm = st["attnfm"]
        last_psP = list(psP_free[0])

        gdma(ap(wproj_sb, 0, [[CH * D, 128], [D, CH], [1, 1024]]),
             ap(d_wproj, 0, [[D, 128], [128 * D, CH], [1, 1024]]),
             waits=qk_release[0] + qk_release[1] + [S.ev(sV)])
        ev_wp = S.ev(dG)
        ev_evac = {}
        for o in range(CH):
            pf = (o % 2) * 2048
            for t in range(NT):
                fw = [ev_wp, last_attnfm]
                if (o - 2, t) in ev_evac:
                    fw.append(ev_evac[(o - 2, t)])
                if o < 2 and t == 0:
                    fw += [psS_free[0], psS_free[1]]
                    fw += psO_free[0] + psO_free[1] + psN_free[0]
                    fw += last_psP
                for d_ in range(CH):
                    S.op("tensor", lambda tn, o=o, t=t, d_=d_, pf=pf: tn.matmul(
                        PS[:, pf + t * 512: pf + t * 512 + 512],
                        wproj_sb[:, d_ * D + o * 128: d_ * D + o * 128 + 128],
                        attn_fm[:, d_ * T + t * 512: d_ * T + t * 512 + 512],
                        start=(d_ == 0), stop=False),
                        waits=fw if d_ == 0 else [])
                S.op("tensor", lambda tn, o=o, t=t, pf=pf: tn.matmul(
                    PS[:, pf + t * 512: pf + t * 512 + 512],
                    bproj_sb[0:1, o * 128:(o + 1) * 128],
                    onesrow_bf[0:1, :], start=False, stop=True),
                    waits=[ev_setup_g], inc=sT)
                ev_mm = S.ev(sT)
                S.op("vector", lambda v, o=o, t=t, pf=pf: v.tensor_tensor(
                    x_all[:, o * T + t * 512: o * T + t * 512 + 512],
                    x_all[:, o * T + t * 512: o * T + t * 512 + 512],
                    PS[:, pf + t * 512: pf + t * 512 + 512], ALU.add),
                    waits=[ev_mm], inc=sV)
                ev_evac[(o, t)] = S.ev(sV)
        return {"last": S.ev(sV), "lastT": S.ev(sT)}

    Z = None

    w1 = make_ffn_w([0, 1, 2], {})
    for e in range(3):
        w1["issue_w"](e)
    ln1 = emit_ln(True, ev_x0, Z, Z, Z)
    ev_cbh1, ev_cbfm1 = emit_router(ln1, Z)
    ffn1 = emit_ffn(w1, ln1, ev_cbh1, ev_cbfm1, Z, Z, [[], []])

    ln2 = emit_ln(False, ffn1["last"], ffn1["lastT"], ffn1["lastT"], ffn1["last"])
    at = emit_attention(ln2, ffn1["lastT"], ffn1["last"], ffn1["lastT"],
                        ffn1["last"])

    w2 = make_ffn_w([1, 2, 0],
                    {0: [at["lastT"], at["last"]], 1: [at["lastT"], at["last"]],
                     2: [at["lastT"], at["last"]]})
    ln3 = emit_ln(True, at["last"], at["lastT"], at["lastT"], at["last"])
    for e in range(3):
        w2["issue_w"](e)
    ev_cbh2, ev_cbfm2 = emit_router(ln3, S.ev(dS))
    ffn2 = emit_ffn(w2, ln3, ev_cbh2, ev_cbfm2, at["lastT"], at["last"],
                    [[at["lastT"], at["last"]], [at["lastT"], at["last"]]])

    for t in range(NT):
        for half in range(2):
            sdma(ap(d_out, half * 4 * 128 * T + t * 512,
                    [[T, 128], [128 * T, 4], [1, 512]]),
                 ap(x_all, half * 4 * T + t * 512,
                    [[CH * T, 128], [T, 4], [1, 512]]),
                 waits=[ffn2["evac"][(3, t, half)]])
    final = S.ev(dS)
    S.op("sync", lambda e: e.nop(), waits=[final])

    with nc.Block() as block:
        S.finalize(block)
    stack.close()
    return nc


def _sel33():
    s = np.zeros((33, 128), np.float32)
    s[0, 0:64] = 1.0
    s[32, 64:128] = 1.0
    return s


def _consts(T):
    inv = 1.0 / (10000.0 ** (np.arange(0, HD, 2, dtype=np.float64) / HD))
    t = np.arange(T, dtype=np.float64)
    fr = t[:, None] * inv[None, :]
    cos = np.cos(fr).astype(np.float32)
    sin = np.sin(fr).astype(np.float32)
    cosx = np.empty((128, T), np.float32)
    sinx = np.empty((128, T), np.float32)
    for b in range(4):
        cosx[b * 32:(b + 1) * 32, :] = cos.T
        sinx[b * 32:(b + 1) * 32, :] = sin.T
    r2 = np.zeros((128, 128), np.float32)
    for blk in range(2):
        o = blk * 64
        for m in range(32):
            r2[o + m + 32, o + m] = -1.0
            r2[o + m, o + m + 32] = 1.0
    return {
        "cosx": cosx, "sinx": sinx, "r2": r2,
        "ident": np.eye(128, dtype=np.float32),
        "onesc": np.ones((128, 1), np.float32),
        "ones11": np.ones((1, 1), np.float32),
        "onesk1": np.ones((1, 128), np.float32),
        "ones65": np.ones((65, 128), np.float32),
        "sel2": _sel33(),
        "onesrow": np.ones((1, 512), np.float32),
    }


def kernel(L0_attn_x, L0_attn_w_qkv, L0_attn_w_proj, L0_attn_b_proj,
           L0_ffn_router_w, L0_ffn_router_b, L0_ffn_expert_w, L0_ffn_expert_b):
    T = L0_attn_x.shape[1]
    nc = build_nc(T)
    xT = np.ascontiguousarray(np.asarray(L0_attn_x).transpose(0, 2, 1))
    shared = {
        "w_qkv": np.ascontiguousarray(np.asarray(L0_attn_w_qkv)),
        "w_proj": np.ascontiguousarray(np.asarray(L0_attn_w_proj)),
        "b_projT": np.ascontiguousarray(np.asarray(L0_attn_b_proj).reshape(1, D)),
        "r_wT": np.ascontiguousarray(np.asarray(L0_ffn_router_w).T),
        "router_bT": np.ascontiguousarray(np.asarray(L0_ffn_router_b).reshape(1, E)),
        "expert_wT": np.ascontiguousarray(np.asarray(L0_ffn_expert_w).transpose(0, 2, 1)),
        "expert_b": np.ascontiguousarray(np.asarray(L0_ffn_expert_b)),
    }
    shared.update(_consts(T))
    in_maps = []
    for b in range(NCORES):
        m = dict(shared)
        m["xT"] = np.ascontiguousarray(xT[b])
        in_maps.append(m)
    res = run_bass_kernel_spmd(nc, in_maps, core_ids=list(range(NCORES)))
    out = np.stack([np.asarray(r["out"]).T for r in res.results], axis=0)
    return np.ascontiguousarray(out.astype(np.float32))
